# version 01 (final)
# baseline (speedup 1.0000x reference)
import math
import numpy as np
import ml_dtypes
import concourse.bass as bass
import concourse.mybir as mybir
from concourse.bass_utils import run_bass_kernel_spmd
from concourse.bass_core_rust import IndirectOffsetOnAxis

F32 = mybir.dt.float32
BF16 = mybir.dt.bfloat16
I32 = mybir.dt.int32
AF = mybir.ActivationFunctionType
ALU = mybir.AluOpType
BF = ml_dtypes.bfloat16

NCORE = 8
DM = 4096
KT = 32
TOK = 1024
G = 8
SEQ = 8192
LSTEP = (8, 16, 24, 32)
SM_SCALE = 1.0 / math.sqrt(128.0)
NEG = -1.0e30
EPS = 1.0e-5

DEBUG_TAPS = False
import os
UPTO = int(os.environ.get("KUPTO", "7"))
KGRP = int(os.environ.get("KGRP", "8"))
KSLM = int(os.environ.get("KSLM", "4"))
KSTP = int(os.environ.get("KSTP", "32"))
KP7 = int(os.environ.get("KP7", "9"))
KSER = int(os.environ.get("KSER", "0"))
KNOROPE = int(os.environ.get("KNOROPE", "0"))
KNOCC = int(os.environ.get("KNOCC", "0"))
KT7 = int(os.environ.get("KT7", "8"))


class _Cut(Exception):
    pass


def chunks_of(core):
    return [core, 15 - core, 16 + core, 31 - core]


def chunk_addr(j):
    if j < 8:
        return j, 0
    if j < 16:
        return 15 - j, 256
    if j < 24:
        return j - 16, 512
    return 31 - j, 768


class Sem:
    def __init__(self, nc, name):
        self.h = nc.alloc_semaphore(name=name)
        self.n = 0

    def bump(self, k):
        self.n += k
        return self.n


class P:
    ENGS = ("sync", "tensor", "vector", "scalar", "gpsimd")

    def __init__(self):
        self.s = {e: [] for e in self.ENGS}
        self.muted = False

    def wait(self, eng, sem, val):
        if self.muted:
            return
        if val and val > 0:
            self.s[eng].append(lambda e, h=sem.h, v=val: e.wait_ge(h, v))

    def dma(self, eng, out, in_, sem, k=16):
        v = sem.bump(k)
        if not self.muted:
            self.s[eng].append(
                lambda e, o=out, i=in_, h=sem.h, kk=k: e.dma_start(out=o, in_=i).then_inc(h, kk))
        return v

    def gather(self, eng, out, dram, idx, sem, k=16):
        v = sem.bump(k)
        if not self.muted:
            self.s[eng].append(lambda e, o=out, d=dram, ix=idx, h=sem.h, kk=k:
                               e.indirect_dma_start(o, None, d, IndirectOffsetOnAxis(ix, 0)).then_inc(h, kk))
        return v

    def scatter(self, eng, dram, idx, in_, sem, k=16):
        v = sem.bump(k)
        if not self.muted:
            self.s[eng].append(lambda e, i=in_, d=dram, ix=idx, h=sem.h, kk=k:
                               e.indirect_dma_start(d, IndirectOffsetOnAxis(ix, 0), i, None).then_inc(h, kk))
        return v

    def op(self, eng, fn, sem=None, k=1):
        if sem is None:
            if not self.muted:
                self.s[eng].append(fn)
            return None
        v = sem.bump(k)
        if not self.muted:
            self.s[eng].append(lambda e, f=fn, h=sem.h, kk=k: f(e).then_inc(h, kk))
        return v


def build_program():
    nc = bass.Bass("TRN2", target_bir_lowering=False)
    p = P()

    def _cut(n):
        if UPTO < n:
            p.muted = True

    x_in = nc.dram_tensor("x_in", [TOK, DM], F32, kind="ExternalInput").ap()
    qcod = nc.dram_tensor("qcod", [DM, DM], I32, kind="ExternalInput").ap()
    kcod = nc.dram_tensor("kcod", [DM, 1024], I32, kind="ExternalInput").ap()
    vcod = nc.dram_tensor("vcod", [DM, 1024], I32, kind="ExternalInput").ap()
    ocod = nc.dram_tensor("ocod", [DM, DM], I32, kind="ExternalInput").ap()
    qscl = nc.dram_tensor("qscl", [32, DM], F32, kind="ExternalInput").ap()
    kscl = nc.dram_tensor("kscl", [32, 1024], F32, kind="ExternalInput").ap()
    vscl = nc.dram_tensor("vscl", [32, 1024], F32, kind="ExternalInput").ap()
    oscl = nc.dram_tensor("oscl", [32, DM], F32, kind="ExternalInput").ap()
    fup = nc.dram_tensor("fup", [16, 256, 1024], F32, kind="ExternalInput").ap()
    fdn = nc.dram_tensor("fdn", [16, 1024, 256], F32, kind="ExternalInput").ap()
    permi = nc.dram_tensor("permi", [128, 32], I32, kind="ExternalInput").ap()
    cost_d = nc.dram_tensor("cost", [128, TOK], F32, kind="ExternalInput").ap()
    sint_d = nc.dram_tensor("sint", [128, TOK], F32, kind="ExternalInput").ap()
    trim_d = nc.dram_tensor("trim", [128, 2048], BF16, kind="ExternalInput").ap()
    bias_d = nc.dram_tensor("biasb", [128, 128], F32, kind="ExternalInput").ap()
    iden_d = nc.dram_tensor("iden", [128, 128], BF16, kind="ExternalInput").ap()
    out_d = nc.dram_tensor("out", [TOK, DM], F32, kind="ExternalOutput").ap()

    kvloc = nc.dram_tensor("kvloc", [2048, 1024], BF16, kind="Internal")
    kvg = nc.dram_tensor("kvg", [8 * 2048, 1024], BF16, kind="Internal")
    yscr = nc.dram_tensor("yscr", [DM, TOK], BF16, kind="Internal")
    fscr = nc.dram_tensor("fscr", [DM, TOK], BF16, kind="Internal")

    taps = {}
    if DEBUG_TAPS:
        taps["tap_q"] = nc.dram_tensor("tap_q", [128, KT * TOK], BF16, kind="ExternalOutput").ap()
        taps["tap_attn"] = nc.dram_tensor("tap_attn", [128, KT * TOK], BF16, kind="ExternalOutput").ap()
        taps["tap_y"] = nc.dram_tensor("tap_y", [128, KT * TOK], BF16, kind="ExternalOutput").ap()
        taps["tap_kv"] = nc.dram_tensor("tap_kv", [2048, 1024], BF16, kind="ExternalOutput").ap()
        taps["tap_xT"] = nc.dram_tensor("tap_xT", [128, KT * TOK], BF16, kind="ExternalOutput").ap()
        taps["tap_kb7"] = nc.dram_tensor("tap_kb7", [128, 1024], BF16, kind="ExternalOutput").ap()
        taps["tap_wbq"] = nc.dram_tensor("tap_wbq", [128, 8192], BF16, kind="ExternalOutput").ap()

    dC = Sem(nc, "dC"); dKV = Sem(nc, "dKV"); cc = Sem(nc, "cc"); dG = Sem(nc, "dG")
    dX2 = [Sem(nc, "dX0"), Sem(nc, "dX1")]
    dW2 = [Sem(nc, "dW0"), Sem(nc, "dW1")]
    dSC = Sem(nc, "dSC"); dWF = Sem(nc, "dWF")
    dST2 = [Sem(nc, "dST0"), Sem(nc, "dST1")]
    dY = Sem(nc, "dY"); dOUT = Sem(nc, "dOUT")
    dF2 = [Sem(nc, "dF0"), Sem(nc, "dF1")]
    dGA2 = [Sem(nc, "dGA0"), Sem(nc, "dGA1")]
    dGS2 = [Sem(nc, "dGS0"), Sem(nc, "dGS1")]
    tA = Sem(nc, "tA"); tSB = Sem(nc, "tSB"); tSC = Sem(nc, "tSC"); tPV = Sem(nc, "tPV")
    vA = Sem(nc, "vA"); vDQ = Sem(nc, "vDQ"); vEV = Sem(nc, "vEV")
    aE = Sem(nc, "aE"); aA = Sem(nc, "aA")

    def phase_sync(engs=("tensor", "vector", "scalar")):
        snap = {"tA": tA.n, "vA": vA.n, "aA": aA.n, "vEV": vEV.n, "vDQ": vDQ.n,
                "aE": aE.n, "tSC": tSC.n, "tPV": tPV.n, "tSB": tSB.n}
        prod = {"tensor": [(tA, snap["tA"]), (tSC, snap["tSC"]), (tPV, snap["tPV"]), (tSB, snap["tSB"])],
                "vector": [(vA, snap["vA"]), (vEV, snap["vEV"]), (vDQ, snap["vDQ"])],
                "scalar": [(aA, snap["aA"]), (aE, snap["aE"])]}
        for e in engs:
            for oth in ("tensor", "vector", "scalar"):
                if oth == e:
                    continue
                for sem, v in prod[oth]:
                    p.wait(e, sem, v)

    import contextlib
    es = contextlib.ExitStack()

    identb = es.enter_context(nc.sbuf_tensor("identb", [128, 128], BF16))
    permb = es.enter_context(nc.sbuf_tensor("permb", [128, 32], I32))
    onevec = es.enter_context(nc.sbuf_tensor("onevec", [128, 1], BF16))
    halfv = es.enter_context(nc.sbuf_tensor("halfv", [1, 128], BF16))
    ones32 = es.enter_context(nc.sbuf_tensor("ones32", [128, 128], F32))
    zerov = es.enter_context(nc.sbuf_tensor("zerov", [1, 128], BF16))
    rop_last = [0]
    rop2_pe = [0]
    xT = es.enter_context(nc.sbuf_tensor("xT", [128, KT * TOK], BF16))
    QA = es.enter_context(nc.sbuf_tensor("QA", [128, KT * TOK], BF16))
    es3 = contextlib.ExitStack()
    cosb = es3.enter_context(nc.sbuf_tensor("cosb", [128, TOK], F32))
    sinb = es3.enter_context(nc.sbuf_tensor("sinb", [128, TOK], F32))
    trimb = es3.enter_context(nc.sbuf_tensor("trimb", [128, 2048], BF16))
    biasb = es3.enter_context(nc.sbuf_tensor("biasbS", [128, 128], F32))
    sclst = es3.enter_context(nc.sbuf_tensor("sclst", [1, 512], BF16))
    cring = es3.enter_context(nc.sbuf_tensor("cring", [128, 512], I32))
    rop1 = es3.enter_context(nc.sbuf_tensor("rop1", [128, TOK], F32))
    rop2 = es3.enter_context(nc.sbuf_tensor("rop2", [128, TOK], F32))
    recb_sb = es3.enter_context(nc.sbuf_tensor("recb_sb", [128, 1024], F32))
    sq_sb = es3.enter_context(nc.sbuf_tensor("sq_sb", [128, 1024], BF16))
    dqflush = es3.enter_context(nc.sbuf_tensor("dqflush", [1, 8], BF16))
    dqflush128 = es3.enter_context(nc.sbuf_tensor("dqflush128", [128, 1], BF16))


    p.dma("sync", cosb[:], cost_d, dC)
    p.dma("sync", sinb[:], sint_d, dC)
    p.dma("sync", trimb[:], trim_d, dC)
    p.dma("sync", biasb[:], bias_d, dC)
    p.dma("sync", identb[:], iden_d, dC)
    c_consts = p.dma("sync", permb[:], permi, dC)
    p.op("vector", lambda e: e.memset(onevec[:], 1.0))
    p.op("vector", lambda e: e.memset(halfv[:], 0.5))
    p.op("vector", lambda e: e.memset(ones32[:], 1.0))
    p.op("vector", lambda e: e.memset(zerov[:], 0.0), vA)
    p.wait("vector", dC, c_consts)
    p.wait("scalar", dC, c_consts)
    p.wait("gpsimd", dC, c_consts)

    _cut(1)

    with nc.sbuf_tensor("xin", [128, 2 * DM], F32) as xin, \
         nc.sbuf_tensor("xc", [128, DM], BF16) as xc, \
         nc.psum_tensor("ptx", [128, 2048], BF16) as ptx:
        cast_done = {}
        tr_last = {}
        evac = {}
        for tt in range(8):
            sl = tt % 2
            if tt >= 2:
                p.wait("sync", vA, cast_done[tt - 2])
            xl = p.dma("sync", xin[:, sl * DM:(sl + 1) * DM], x_in[128 * tt:128 * (tt + 1), :], dX2[sl])
            p.wait("vector", dX2[sl], xl)
            if tt >= 1:
                p.wait("vector", tA, tr_last[tt - 1])
            cast_done[tt] = p.op(
                "vector", lambda e, s=sl: e.tensor_copy(xc[:], xin[:, s * DM:(s + 1) * DM]), vA)
            p.wait("tensor", vA, cast_done[tt])
            if tt == 0:
                p.wait("tensor", dC, c_consts)
            for ft in range(KT):
                idx = tt * KT + ft
                ps = idx % 2
                if idx >= 2:
                    p.wait("tensor", vA, evac[idx - 2])
                tr = p.op("tensor", lambda e, f=ft, q=ps: e.transpose(
                    ptx[:, 1024 * q:1024 * q + 128], xc[:, 128 * f: 128 * (f + 1)], identb[:]), tA)
                p.wait("vector", tA, tr)
                evac[idx] = p.op("vector", lambda e, f=ft, t=tt, q=ps: e.tensor_copy(
                    xT[:, 1024 * f + 128 * t: 1024 * f + 128 * (t + 1)], ptx[:, 1024 * q:1024 * q + 128]), vA)
            tr_last[tt] = tr
        x_done = evac[8 * KT - 1]

    if DEBUG_TAPS:
        p.wait("gpsimd", vA, x_done)
        tv = p.dma("gpsimd", taps["tap_xT"], xT[:], dG)
        p.wait("tensor", dG, tv)
        p.wait("vector", dG, tv)
        p.wait("sync", dG, tv)

    _cut(2)

    def feeder_stageA(cod_ap, scl_row, scl_col, k, colbase, W, roff, pscl_ap,
                      ring_prev_sb, ring_prev_ps, final=False):
        par = (roff // 256) % 2
        if ring_prev_sb:
            p.wait("sync", vDQ, ring_prev_sb)
        cw = p.dma("sync", cring[:, roff: roff + W],
                   cod_ap[128 * k:128 * (k + 1), colbase:colbase + W], dW2[par])
        sw = p.dma("sync", sclst[:, roff: roff + W],
                   scl_row[k:k + 1, scl_col: scl_col + W], dW2[par])
        p.wait("tensor", dW2[par], sw)
        if ring_prev_ps:
            p.wait("tensor", vDQ, ring_prev_ps)
        p.op("tensor", lambda e, ro=roff, w=W, pa=pscl_ap: e.matmul(
            pa, halfv[:], sclst[:, ro: ro + w], start=True, stop=True,
            skip_group_check=True), None if final else tSB)
        if final:
            p.op("tensor", lambda e, ro=roff, w=W, pa=pscl_ap: e.matmul(
                pa, halfv[:], sclst[:, ro: ro + w], start=True, stop=True,
                skip_group_check=True), tSB)
        return tSB.n

    def feeder_stageB(W, roff, pscl_ap, wdst, sb_val, pe_prev=None, slack=False):
        p.wait("vector", tSB, sb_val)
        if pe_prev:
            p.wait("vector", tA, pe_prev)
        if KSER:
            p.wait("vector", tA, tA.n)
            p.wait("vector", tSC, tSC.n)
            p.wait("vector", aE, aE.n)
        dq = p.op("vector", lambda e, ro=roff, w=W, pa=pscl_ap, wd=wdst: e.scalar_tensor_tensor(
            wd, cring[:, ro: ro + w], -8.0, pa, op0=ALU.add, op1=ALU.mult), vDQ)
        return dq

    def block_feeder_factory(cod_ap, scl_ap, wbuf_ap_fn, pscl, tile_dq, blk_dq, blk_cons,
                             nblocks, W=256, scl_col_fn=None, colbase_fn=None):
        st = {"pend": None}
        if scl_col_fn is None:
            scl_col_fn = lambda qb: (256 * qb) % 1024 + 1024 * ((qb // 4) % 2)
        if colbase_fn is None:
            colbase_fn = lambda qb: 256 * qb

        def flush_pend():
            if st["pend"] is None:
                return
            qb, k, sbv, fin = st["pend"]
            st["pend"] = None
            gi = qb * KT + k
            slot = gi % 2
            pe_prev = blk_cons.get(qb - 2) if k == 0 else None
            tile_dq[gi] = feeder_stageB(W, 256 * slot, pscl[:, 512 * slot: 512 * slot + W],
                                        wbuf_ap_fn(qb, k), sbv, pe_prev=pe_prev, slack=fin)
            if k == KT - 1:
                blk_dq[qb] = tile_dq[gi]

        def feeder(qb):
            for k in range(KT):
                gi = qb * KT + k
                slot = gi % 2
                fin = (qb == nblocks - 1 and k == KT - 1)
                sbv = feeder_stageA(cod_ap, scl_ap, scl_col_fn(qb), k, colbase_fn(qb), W, 256 * slot,
                                    pscl[:, 512 * slot: 512 * slot + W],
                                    tile_dq.get(gi - 2), tile_dq.get(gi - 2), final=fin)
                flush_pend()
                st["pend"] = (qb, k, sbv, fin)

        return feeder, flush_pend

    def pe_fence(out_ap, rhs_ap, sem, m128=True):
        lhs = zerov[:] if m128 else zerov[0:1, 0:1]
        return p.op("tensor", lambda e, o=out_ap, r=rhs_ap, L=lhs: e.matmul(
            o, L, r, start=False, stop=True, skip_group_check=True), sem)

    with nc.sbuf_tensor("sclbuf", [32, 1024], F32) as sclbuf, \
         nc.sbuf_tensor("sclbf", [32, 2048], BF16) as sclbf, \
         nc.sbuf_tensor("wkv", [128, 2 * 4096], BF16) as wkv, \
         nc.sbuf_tensor("kvbf", [128, 2 * 1024], BF16) as kvbf, \
         nc.sbuf_tensor("vtt", [128, 2 * 128], BF16) as vtt, \
         nc.psum_tensor("pkv", [128, 2 * 1024], F32) as pkv, \
         nc.psum_tensor("psclA", [128, 1024], F32) as psclA, \
         nc.psum_tensor("ptv", [128, 2048], BF16) as ptv:

        p.wait("sync", vA, x_done)
        kw = p.dma("sync", sclbuf[:], kscl, dSC)
        p.wait("vector", dSC, kw)
        kca = p.op("vector", lambda e, sf=sclbf, sb=sclbuf: e.tensor_copy(sf[:, 0:1024], sb[:]), vA)
        p.wait("sync", vA, kca)
        vw = p.dma("sync", sclbuf[:], vscl, dSC)
        p.wait("vector", dSC, vw)
        p.op("vector", lambda e, sf=sclbf, sb=sclbuf: e.tensor_copy(sf[:, 1024:2048], sb[:]), vA)

        def kv_proj(cod_ap, scl_colbase, is_k, store_fn, prev_stores, prev_seeds=None):
            done = {}
            store_v = dict(prev_stores)
            tile_dq = {-1: vDQ.n, -2: vDQ.n}
            blk_dq = {}
            blk_cons = dict(prev_seeds or {})
            feeder, flush = block_feeder_factory(
                cod_ap, sclbf.ap(),
                lambda gt, k: wkv[:, (gt % 2) * 4096 + 128 * k:(gt % 2) * 4096 + 128 * (k + 1)],
                psclA, tile_dq, blk_dq, blk_cons, 8, W=128,
                scl_col_fn=lambda gt: scl_colbase + 128 * gt,
                colbase_fn=lambda gt: 128 * gt)

            def consumer(gt):
                pp = gt % 2
                p.wait("tensor", vDQ, blk_dq.get(gt + 1, tile_dq.get((gt + 1) * KT + 30, blk_dq[gt])))
                p.wait("tensor", vDQ, blk_dq[gt])
                p.wait("tensor", vA, x_done)
                if gt >= 2:
                    p.wait("tensor", vA, done[gt - 2])
                for k in range(KT):
                    for half in range(2):
                        p.op("tensor", lambda e, g=gt, h=half, q=pp, kk=k: e.matmul(
                            pkv[:, q * 1024 + h * 512: q * 1024 + (h + 1) * 512],
                            wkv[:, (g % 2) * 4096 + 128 * kk:(g % 2) * 4096 + 128 * (kk + 1)],
                            xT[:, 1024 * kk + h * 512: 1024 * kk + (h + 1) * 512],
                            start=(kk == 0), stop=(kk == KT - 1), skip_group_check=True),
                            tA if (k == KT - 1 and half == 1) else None)
                fv = pe_fence(pkv[:, pp * 1024 + 512: pp * 1024 + 768], xT[0:1, 0:256], tA)
                p.wait("vector", tA, fv)
                sv = store_v.get(gt - 2)
                if sv:
                    p.wait("vector", dST2[sv[0]], sv[1])
                pk = pkv[:, pp * 1024:(pp + 1) * 1024]
                if is_k:
                    p.wait("vector", vA, rop_last[0])
                    p.op("vector", lambda e, q=pk: e.tensor_tensor(rop1[0:64, :], q[64:128, :], sinb[0:64, :], op=ALU.mult))
                    p.op("vector", lambda e, q=pk: e.tensor_tensor(rop1[64:128, :], q[0:64, :], sinb[64:128, :], op=ALU.mult))
                    rw = p.op("vector", lambda e, q=pk: e.tensor_tensor(rop2[:], q[:], cosb[:], op=ALU.mult), vA)
                    p.wait("vector", vA, rw)
                    done[gt] = p.op("vector", lambda e, q=pp: e.tensor_tensor(
                        kvbf[:, q * 1024:(q + 1) * 1024], rop1[:], rop2[:], op=ALU.add), vA)
                    rop_last[0] = done[gt]
                else:
                    done[gt] = p.op("vector", lambda e, q=pp, pz=pk: e.tensor_copy(
                        kvbf[:, q * 1024:(q + 1) * 1024], pz), vA)
                if DEBUG_TAPS and is_k and gt == 7:
                    p.wait("gpsimd", vA, done[gt])
                    tv7 = p.dma("gpsimd", taps["tap_kb7"], kvbf[:, 1024:2048], dG)
                    p.wait("scalar", dG, tv7)
                    p.wait("tensor", dG, tv7)
                    p.wait("vector", dG, tv7)
                store_v[gt] = store_fn(gt, pp, done[gt])
                blk_cons[gt] = tA.n

            feeder(0)
            for gt in range(8):
                if gt + 1 < 8:
                    feeder(gt + 1)
                else:
                    flush()
                consumer(gt)
            return {-2: blk_cons[6], -1: blk_cons[7]}

        def k_store(gt, pp, rd):
            p.wait("scalar", vA, rd)
            return (pp, p.dma("scalar", kvloc.ap()[128 * gt:128 * (gt + 1), :],
                              kvbf[:, pp * 1024:(pp + 1) * 1024], dST2[pp]))

        vtt_ev = {}

        def v_store(vt, pp, rd):
            last = None
            for tt in range(8):
                idx = vt * 8 + tt
                pz = idx % 2
                p.wait("tensor", vA, rd if tt == 0 else 0)
                if idx >= 2:
                    p.wait("tensor", vA, vtt_ev[idx - 2])
                trv = p.op("tensor", lambda e, t=tt, q=pp, z=pz: e.transpose(
                    ptv[:, 1024 * z:1024 * z + 128],
                    kvbf[:, q * 1024 + 128 * t: q * 1024 + 128 * (t + 1)], identb[:]), tA)
                p.wait("vector", tA, trv)
                if idx >= 2:
                    p.wait("vector", dST2[pz], vstore_d.get(idx - 2, 0))
                vtt_ev[idx] = p.op("vector", lambda e, z=pz: e.tensor_copy(
                    vtt[:, 128 * z:128 * (z + 1)], ptv[:, 1024 * z:1024 * z + 128]), vA)
                p.wait("scalar", vA, vtt_ev[idx])
                vstore_d[idx] = p.dma(
                    "scalar", kvloc.ap()[1024 + 128 * tt:1024 + 128 * (tt + 1), 128 * vt:128 * (vt + 1)],
                    vtt[:, 128 * pz:128 * (pz + 1)], dST2[pz])
                last = (pz, vstore_d[idx])
            return last

        vstore_d = {}
        kseeds = kv_proj(kcod, 0, True, k_store, {})
        phase_sync()
        p.wait("vector", dST2[0], dST2[0].n)
        p.wait("vector", dST2[1], dST2[1].n)
        kv_proj(vcod, 1024, False, v_store, {}, prev_seeds=kseeds)
        kv_stores = (dST2[0].n, dST2[1].n)

    p.wait("gpsimd", dST2[0], kv_stores[0])
    p.wait("gpsimd", dST2[1], kv_stores[1])
    if KNOCC:
        ccv = 0
    else:
        ccv = p.op("gpsimd", lambda e: e.collective_compute(
            "AllGather", ALU.bypass, replica_groups=[list(range(NCORE))],
            ins=[kvloc.ap()], outs=[kvg.ap()]), cc)
    if DEBUG_TAPS:
        p.wait("gpsimd", cc, ccv)
        p.dma("gpsimd", taps["tap_kv"], kvloc.ap(), dG)

    _cut(3)

    phase_sync()
    rope_q = {}
    p.wait("sync", cc, cc.n)
    p.wait("vector", cc, cc.n)
    p.wait("tensor", cc, cc.n)
    p.wait("scalar", cc, cc.n)
    p.wait("vector", dST2[0], dST2[0].n)
    p.wait("vector", dST2[1], dST2[1].n)
    p.wait("sync", vDQ, vDQ.n)
    p.wait("sync", dST2[0], dST2[0].n)
    p.wait("sync", dST2[1], dST2[1].n)
    with nc.sbuf_tensor("sclbufQ", [32, 1024], F32) as sclbuf, \
         nc.sbuf_tensor("sclbfQ", [32, 2048], BF16) as sclbf, \
         nc.sbuf_tensor("wbq", [128, 2 * 8192], BF16) as wbq, \
         nc.psum_tensor("pq", [128, 2 * 1024], F32) as pq, \
         nc.psum_tensor("psclB", [128, 1024], F32) as psclB:

        def scl_loader(dram, tile_dq_ref):
            state = {"prev_cast": 0}

            def go(j):
                p.wait("sync", vA, state["prev_cast"])
                if j >= 2:
                    p.wait("sync", vDQ, tile_dq_ref.get(4 * (j - 2) * KT + 4 * KT - 1, 0))
                lw = p.dma("sync", sclbuf[:], dram[:, 1024 * j:1024 * (j + 1)], dSC)
                p.wait("vector", dSC, lw)
                state["prev_cast"] = p.op(
                    "vector", lambda e, q=j % 2, sf=sclbf, sb=sclbuf: e.tensor_copy(sf[:, 1024 * q:1024 * (q + 1)], sb[:]), vA)
                return state["prev_cast"]
            return go


        blk_dq = {}
        tile_dq = {-1: vDQ.n, -2: vDQ.n}
        blk_cons = {}
        q_scl = scl_loader(qscl, tile_dq)
        q_scl(0)
        scl_ready = q_scl(1)

        q_feeder, q_flush = block_feeder_factory(
            qcod, sclbf.ap(),
            lambda qb, k: wbq[:, (qb % 2) * 8192 + 256 * k:(qb % 2) * 8192 + 256 * (k + 1)],
            psclB, tile_dq, blk_dq, blk_cons, 16)

        def q_consumer(qb):
            for hh in range(2):
                ht = 2 * qb + hh
                pp = ht % 2
                if KSER:
                    p.wait("tensor", vDQ, vDQ.n)
                    p.wait("tensor", vA, vA.n)
                p.wait("tensor", vDQ, blk_dq.get(qb + 1, tile_dq.get((qb + 1) * KT + 30, blk_dq[qb])))
                p.wait("tensor", vDQ, blk_dq[qb])
                if ht >= 2:
                    p.wait("tensor", vA, rope_q[ht - 2])
                for k in range(KT):
                    for half in range(2):
                        p.op("tensor", lambda e, q=qb, kk=k, h=hh, hf=half, pz=pp: e.matmul(
                            pq[:, pz * 1024 + hf * 512: pz * 1024 + (hf + 1) * 512],
                            wbq[:, (q % 2) * 8192 + 256 * kk + 128 * h: (q % 2) * 8192 + 256 * kk + 128 * (h + 1)],
                            xT[:, 1024 * kk + hf * 512: 1024 * kk + (hf + 1) * 512],
                            start=(kk == 0), stop=(kk == KT - 1), skip_group_check=True),
                            tA if (k == KT - 1 and half == 1) else None)
                fv = pe_fence(pq[:, pp * 1024 + 512: pp * 1024 + 768], xT[0:1, 0:256], tA)
                p.wait("vector", tA, fv)
                qp = pq[:, pp * 1024:(pp + 1) * 1024]
                p.wait("vector", vA, rop_last[0])
                if KNOROPE:
                    rope_q[ht] = p.op("vector", lambda e, q=qp, t=ht: e.tensor_copy(
                        QA[:, 1024 * t:1024 * (t + 1)], q), vA)
                else:
                    p.op("vector", lambda e, q=qp: e.tensor_tensor(rop1[0:64, :], q[64:128, :], sinb[0:64, :], op=ALU.mult))
                    p.op("vector", lambda e, q=qp: e.tensor_tensor(rop1[64:128, :], q[0:64, :], sinb[64:128, :], op=ALU.mult))
                    rw = p.op("vector", lambda e, q=qp: e.tensor_tensor(rop2[:], q[:], cosb[:], op=ALU.mult), vA)
                    p.wait("vector", vA, rw)
                    rope_q[ht] = p.op("vector", lambda e, t=ht: e.tensor_tensor(
                        QA[:, 1024 * t:1024 * (t + 1)], rop1[:], rop2[:], op=ALU.add), vA)
                    rop_last[0] = rope_q[ht]
            blk_cons[qb] = tA.n

        p.wait("sync", vA, scl_ready)
        q_feeder(0)
        for qb in range(16):
            if qb + 1 < 16:
                if (qb + 1) in (5, 9):
                    q_scl(2 + (qb + 1 - 5) // 4)
                q_feeder(qb + 1)
            else:
                q_flush()
            q_consumer(qb)
        q_all_done = rope_q[31]

    if DEBUG_TAPS:
        p.wait("gpsimd", vA, q_all_done)
        tv = p.dma("gpsimd", taps["tap_q"], QA[:], dG)
        p.wait("tensor", dG, tv)
        p.wait("vector", dG, tv)
        p.wait("scalar", dG, tv)

    _cut(4)

    phase_sync()
    with nc.sbuf_tensor("kbuf", [128, 31 * 256], BF16) as kbuf, \
         nc.sbuf_tensor("vbuf", [128, 31 * 256], BF16) as vbuf, \
         nc.sbuf_tensor("kdiag", [128, 4 * 256], BF16) as kdiag, \
         nc.sbuf_tensor("vdiag", [128, 4 * 256], BF16) as vdiag, \
         nc.sbuf_tensor("prb", [128, 4 * 1024], BF16) as prb, \
         nc.psum_tensor("pscore", [128, 2048], F32) as pscore, \
         nc.psum_tensor("pacc", [128, 1024], F32) as pacc, \
         nc.psum_tensor("pden", [1, 1024], F32) as pden:

        if not KNOCC:
            p.wait("sync", cc, ccv)
        p.wait("sync", tA, tA.n)
        p.wait("sync", vA, vA.n)
        p.wait("tensor", vA, q_all_done)
        grp_sc_end = [0] * G
        grp_pv_end = [0] * G
        kvg_ap = kvg.ap()
        kvl_ap = kvloc.ap()

        for g in range(G):
            if g >= KGRP:
                p.muted = True
            if g >= 1:
                p.wait("sync", tSC, grp_sc_end[g - 1])
                p.wait("sync", tPV, grp_pv_end[g - 1])
            for j in range(31):
                r, co = chunk_addr(j)
                p.dma("sync", kbuf[:, 256 * j:256 * (j + 1)],
                      kvg_ap[2048 * r + 128 * g: 2048 * r + 128 * (g + 1), co:co + 256], dKV)
                p.dma("sync",
                      vbuf[:, 256 * j:256 * (j + 1)].rearrange("p (a d) -> p a d", d=128),
                      kvg_ap[2048 * r + 1024 + co: 2048 * r + 1024 + co + 256,
                             128 * g:128 * (g + 1)].rearrange("(a p) d -> p a d", p=128), dKV)
            for i in range(4):
                p.dma("sync", kdiag[:, 256 * i:256 * (i + 1)],
                      kvl_ap[128 * g:128 * (g + 1), 256 * i:256 * (i + 1)], dKV)
                p.dma("sync",
                      vdiag[:, 256 * i:256 * (i + 1)].rearrange("p (a d) -> p a d", d=128),
                      kvl_ap[1024 + 256 * i:1024 + 256 * (i + 1),
                             128 * g:128 * (g + 1)].rearrange("(a p) d -> p a d", p=128), dKV)
            kv_ld = dKV.n

            for i in range(4):
                if g == KGRP - 1 and i >= KSLM:
                    p.muted = True
                L = LSTEP[i]
                exp_v = {}
                sc_v = {}
                pv_v = {}
                mask_v = {}
                prev_ev = vEV.n

                def emit_scores(s):
                    for kt in range(2):
                        if s >= 1:
                            p.wait("tensor", aE, exp_v[(s - 1, kt)])
                        else:
                            p.wait("tensor", vEV, prev_ev)
                            p.wait("tensor", dKV, kv_ld)
                        for hp in range(2):
                            h0 = 4 * g + 2 * hp
                            kT_ap = (kdiag[:, 256 * i + 128 * kt: 256 * i + 128 * (kt + 1)] if s == 0
                                     else kbuf[:, 256 * (s - 1) + 128 * kt: 256 * (s - 1) + 128 * (kt + 1)])
                            rhs = QA.ap().rearrange("p (h t) -> p h t", t=1024)[:, h0:h0 + 2, 256 * i:256 * (i + 1)]
                            sc_v[(s, kt)] = p.op("tensor", lambda e, a=kT_ap, r=rhs, k=kt, q=hp: e.matmul(
                                pscore[:, 1024 * k + 512 * q: 1024 * k + 512 * (q + 1)],
                                a, r, start=True, stop=True, skip_group_check=True), tSC)

                def emit_exp(s):
                    for kt in range(2):
                        p.wait("scalar", tSC, sc_v[(s, kt)])
                        if s == 0:
                            p.wait("scalar", vEV, mask_v[kt])
                            p.wait("scalar", vEV, prev_ev)
                        if s >= 2:
                            p.wait("scalar", tPV, pv_v[(s - 2, kt)])
                        pb = (s % 2) * 2 + kt
                        exp_v[(s, kt)] = p.op("scalar", lambda e, k=kt, b=pb, c=32 * i + s: e.activation(
                            prb[:, 1024 * b:1024 * (b + 1)], pscore[:, 1024 * k:1024 * (k + 1)],
                            AF.Exp, bias=biasb[:, c:c + 1], scale=SM_SCALE), aE)

                def emit_pv(s):
                    for kt in range(2):
                        p.wait("tensor", aE, exp_v[(s, kt)])
                        pb = (s % 2) * 2 + kt
                        v_ap = (vdiag[:, 256 * i + 128 * kt: 256 * i + 128 * (kt + 1)] if s == 0
                                else vbuf[:, 256 * (s - 1) + 128 * kt: 256 * (s - 1) + 128 * (kt + 1)])
                        st = (s == 0 and kt == 0)
                        for hp in range(2):
                            p.op("tensor", lambda e, v=v_ap, b=pb, q=hp, z=st: e.matmul(
                                pacc[:, 512 * q:512 * (q + 1)], v,
                                prb[:, 1024 * b + 512 * q: 1024 * b + 512 * (q + 1)],
                                start=z, stop=False, skip_group_check=True))
                        for hp in range(2):
                            pv_v[(s, kt)] = p.op("tensor", lambda e, b=pb, q=hp, z=st: e.matmul(
                                pden[:, 512 * q:512 * (q + 1)], onevec[:],
                                prb[:, 1024 * b + 512 * q: 1024 * b + 512 * (q + 1)],
                                start=z, stop=False, skip_group_check=True), tPV)

                emit_scores(0)
                for kt in range(2):
                    p.wait("vector", tSC, sc_v[(0, kt)])
                    mask_v[kt] = p.op("vector", lambda e, k=kt: e.tensor_tensor(
                        pscore[:, 1024 * k:1024 * (k + 1)], pscore[:, 1024 * k:1024 * (k + 1)],
                        trimb[:, 1024 * k:1024 * (k + 1)], op=ALU.add), vEV)
                emit_exp(0)
                for s in range(1, L):
                    if g == KGRP - 1 and i == KSLM - 1 and s >= KSTP:
                        p.muted = True
                    emit_scores(s)
                    emit_exp(s)
                    emit_pv(s - 1)
                emit_pv(L - 1)

                fden = pe_fence(pden[0:1, 512:768], QA[0:1, 0:256], tPV, m128=False)
                p.wait("vector", tPV, fden)
                p.wait("vector", vA, rop_last[0])
                p.wait("vector", tA, rop2_pe[0])
                dnc = p.op("vector", lambda e: e.tensor_copy(rop1[0:1, :], pden[:]), vA)
                p.wait("vector", vA, dnc)
                rcv = p.op("vector", lambda e: e.reciprocal(rop2[0:1, :], rop1[0:1, :]), vEV)
                rop_last[0] = vA.n
                p.wait("tensor", vEV, rcv)
                p.wait("tensor", aE, exp_v[(L - 1, 1)])
                for half in range(2):
                    p.op("tensor", lambda e, h=half: e.matmul(
                        pscore[:, 512 * h:512 * (h + 1)], ones32[0:1, :], rop2[0:1, 512 * h:512 * (h + 1)],
                        start=True, stop=True, skip_group_check=True),
                        tA if half == 1 else None)
                bcv = tA.n
                rop2_pe[0] = bcv
                p.wait("vector", tA, bcv)
                rbc = p.op("vector", lambda e: e.tensor_copy(recb_sb[:], pscore[:, 0:1024]), vA)
                p.wait("vector", vA, rbc)
                for h in range(4):
                    hab = 4 * g + h
                    p.op("vector", lambda e, hh=h, t=hab, ii=i: e.tensor_tensor(
                        QA[:, 1024 * t + 256 * ii: 1024 * t + 256 * (ii + 1)],
                        pacc[:, 256 * hh:256 * (hh + 1)], recb_sb[:, 256 * hh:256 * (hh + 1)],
                        op=ALU.mult), vEV)
            grp_sc_end[g] = tSC.n
            grp_pv_end[g] = tPV.n
        attn_done_v = vEV.n

    if DEBUG_TAPS:
        p.wait("gpsimd", vEV, attn_done_v)
        p.dma("gpsimd", taps["tap_attn"], QA[:], dG)

    _cut(5)

    phase_sync()
    with nc.sbuf_tensor("sclbufO", [32, 1024], F32) as sclbuf, \
         nc.sbuf_tensor("sclbfO", [32, 2048], BF16) as sclbf, \
         nc.sbuf_tensor("wbo", [128, 2 * 8192], BF16) as wbo, \
         nc.psum_tensor("po", [128, 1024], F32) as po, \
         nc.psum_tensor("psclC", [128, 1024], F32) as psclC, \
         nc.psum_tensor("psums", [1, 1024], F32) as psums, \
         nc.psum_tensor("psumsq", [1, 1024], F32) as psumsq:

        p.wait("vector", vEV, attn_done_v)
        p.wait("sync", tSC, tSC.n)
        p.wait("sync", tPV, tPV.n)
        p.wait("sync", vEV, attn_done_v)
        ostate = {"prev_cast": 0}

        def o_scl(j):
            p.wait("sync", vA, ostate["prev_cast"])
            if j >= 2:
                p.wait("sync", vDQ, o_tile_dq.get(4 * (j - 2) * KT + 4 * KT - 1, 0))
            lw = p.dma("sync", sclbuf[:], oscl[:, 1024 * j:1024 * (j + 1)], dSC)
            p.wait("vector", dSC, lw)
            ostate["prev_cast"] = p.op(
                "vector", lambda e, q=j % 2, sf=sclbf, sb=sclbuf: e.tensor_copy(sf[:, 1024 * q:1024 * (q + 1)], sb[:]), vA)
            return ostate["prev_cast"]

        o_scl(0)
        oscl_ready = o_scl(1)

        o_dq = {}
        o_tile_dq = {-1: vDQ.n, -2: vDQ.n}
        o_cons = {}
        res_v = {}
        sq_mm = {}

        o_feeder, o_flush = block_feeder_factory(
            ocod, sclbf.ap(),
            lambda ob, k: wbo[:, (ob % 2) * 8192 + 256 * k:(ob % 2) * 8192 + 256 * (k + 1)],
            psclC, o_tile_dq, o_dq, o_cons, 16)

        def o_consumer(ob):
            for hh in range(2):
                mt = 2 * ob + hh
                p.wait("tensor", vDQ, o_dq.get(ob + 1, o_tile_dq.get((ob + 1) * KT + 30, o_dq[ob])))
                p.wait("tensor", vDQ, o_dq[ob])
                if mt >= 1:
                    p.wait("tensor", vA, res_v[mt - 1])
                for k in range(KT):
                    for half in range(2):
                        p.op("tensor", lambda e, o=ob, kk=k, h=hh, hf=half: e.matmul(
                            po[:, hf * 512:(hf + 1) * 512],
                            wbo[:, (o % 2) * 8192 + 256 * kk + 128 * h:(o % 2) * 8192 + 256 * kk + 128 * (h + 1)],
                            QA[:, 1024 * kk + hf * 512: 1024 * kk + (hf + 1) * 512],
                            start=(kk == 0), stop=(kk == KT - 1), skip_group_check=True),
                            tA if (k == KT - 1 and half == 1) else None)
                fv = pe_fence(po[:, 512:768], QA[0:1, 0:256], tA)
                p.wait("vector", tA, fv)
                res_v[mt] = p.op("vector", lambda e, m=mt: e.tensor_tensor(
                    xT[:, 1024 * m:1024 * (m + 1)], po[:], xT[:, 1024 * m:1024 * (m + 1)], op=ALU.add), vA)
                p.wait("scalar", vA, res_v[mt])
                if mt >= 1:
                    p.wait("scalar", tA, sq_mm[mt - 1])
                sqv = p.op("scalar", lambda e, m=mt: e.activation(
                    sq_sb[:], xT[:, 1024 * m:1024 * (m + 1)], AF.Square), aA)
                p.wait("tensor", vA, res_v[mt])
                for half in range(2):
                    p.op("tensor", lambda e, m=mt, h=half: e.matmul(
                        psums[:, h * 512:(h + 1) * 512], onevec[:],
                        xT[:, 1024 * m + h * 512: 1024 * m + (h + 1) * 512],
                        start=(m == 0), stop=(m == KT - 1), skip_group_check=True))
                p.wait("tensor", aA, sqv)
                for half in range(2):
                    p.op("tensor", lambda e, m=mt, h=half: e.matmul(
                        psumsq[:, h * 512:(h + 1) * 512], onevec[:],
                        sq_sb[:, h * 512:(h + 1) * 512],
                        start=(m == 0), stop=(m == KT - 1), skip_group_check=True),
                        tA)
                sq_mm[mt] = tA.n
            o_cons[ob] = tA.n

        p.wait("sync", vA, oscl_ready)
        o_feeder(0)
        for ob in range(16):
            if ob + 1 < 16:
                if (ob + 1) in (5, 9):
                    o_scl(2 + (ob + 1 - 5) // 4)
                o_feeder(ob + 1)
            else:
                o_flush()
            o_consumer(ob)
        stats_done = tA.n

        A = rop1[0:1, 0:1024]
        B = rop2[0:1, 0:1024]
        C = recb_sb[0:1, 0:1024]
        fvs = pe_fence(psums[0:1, 512:768], QA[0:1, 0:256], tA, m128=False)
        fvq = pe_fence(psumsq[0:1, 512:768], QA[0:1, 0:256], tA, m128=False)
        p.wait("vector", tA, fvq)
        p.wait("vector", vA, rop_last[0])
        p.wait("vector", tA, rop2_pe[0])
        chain = [
            lambda e: e.tensor_copy(A, psums[:]),
            lambda e: e.tensor_copy(C, psumsq[:]),
            lambda e: e.tensor_scalar(A, A, 1.0 / DM, None, op0=ALU.mult),
            lambda e: e.tensor_scalar(C, C, 1.0 / DM, None, op0=ALU.mult),
            lambda e: e.tensor_tensor(B, A, A, op=ALU.mult),
            lambda e: e.tensor_tensor(B, C, B, op=ALU.subtract),
            lambda e: e.tensor_scalar(B, B, EPS, None, op0=ALU.add),
        ]
        lf = 0
        for fn in chain:
            lf = p.op("vector", fn, vA)
            p.wait("vector", vA, lf)
        p.wait("scalar", vA, lf)
        sqv2 = p.op("scalar", lambda e: e.activation(B, B, AF.Sqrt), aA)
        p.wait("vector", aA, sqv2)
        rcp = p.op("vector", lambda e: e.reciprocal(B, B), vA)
        p.wait("vector", vA, rcp)
        ln1f = p.op("vector", lambda e: e.scalar_tensor_tensor(
            C, A, -1.0, B, op0=ALU.mult, op1=ALU.mult), vA)

    phase_sync()
    y_v = {}
    with nc.psum_tensor("pbc1", [128, 1024], F32) as pbc1, \
         nc.psum_tensor("pbc2", [128, 1024], F32) as pbc2:
        p.wait("tensor", vA, ln1f)
        for half in range(2):
            p.op("tensor", lambda e, h=half: e.matmul(
                pbc1[:, h * 512:(h + 1) * 512], ones32[0:1, :], rop2[0:1, h * 512:(h + 1) * 512],
                start=True, stop=True, skip_group_check=True))
        for half in range(2):
            p.op("tensor", lambda e, h=half: e.matmul(
                pbc2[:, h * 512:(h + 1) * 512], ones32[0:1, :], recb_sb[0:1, h * 512:(h + 1) * 512],
                start=True, stop=True, skip_group_check=True),
                tA if half == 1 else None)
        bc2 = pe_fence(pbc2[:, 512:768], QA[0:1, 0:256], tA)
        p.wait("vector", tA, bc2)
        for mt in range(KT):
            p.wait("vector", vA, y_v[mt - 1] if mt >= 1 else ln1f)
            yw = p.op("vector", lambda e, m=mt: e.tensor_tensor(
                rop1[:], xT[:, 1024 * m:1024 * (m + 1)], pbc1[:], op=ALU.mult), vA)
            p.wait("vector", vA, yw)
            y_v[mt] = p.op("vector", lambda e, m=mt: e.tensor_tensor(
                QA[:, 1024 * m:1024 * (m + 1)], rop1[:], pbc2[:], op=ALU.add), vA)

    if DEBUG_TAPS:
        p.wait("gpsimd", vA, y_v[KT - 1])
        p.dma("gpsimd", taps["tap_y"], QA[:], dG)

    _cut(6)

    ys_v = {}
    for mt in range(KT):
        p.wait("scalar", vA, y_v[mt])
        ys_v[mt] = p.dma("scalar", yscr.ap()[128 * mt:128 * (mt + 1), :], QA[:, 1024 * mt:1024 * (mt + 1)], dY)
    y_stored = ys_v[KT - 1]

    phase_sync()
    es3.close()
    with nc.sbuf_tensor("upf", [128, 2048], F32) as upf, \
         nc.sbuf_tensor("dnf", [128, 2048], F32) as dnf, \
         nc.sbuf_tensor("upb", [128, 2 * 2048], BF16) as upb, \
         nc.sbuf_tensor("dnb", [128, 2 * 2048], BF16) as dnb, \
         nc.sbuf_tensor("ypg", [128, 2 * 2048], BF16) as ypg, \
         nc.sbuf_tensor("tb", [128, 8192], BF16) as tb, \
         nc.sbuf_tensor("obf", [128, 2 * 2048], BF16) as obf, \
         nc.psum_tensor("pup", [128, 2 * 1024], F32) as pup, \
         nc.psum_tensor("pdn", [128, 1024], F32) as pdn:

        wld = {}; cst = {}; gat = {}; scat = {}; dcons = {}; gcons = {}; down_done = {}
        p.wait("gpsimd", dY, y_stored)

        def ffn_wload(h):
            r2 = h % 2
            if h == 0:
                p.wait("sync", vA, vA.n)
                p.wait("sync", tA, tA.n)
            if h >= 1:
                p.wait("sync", vA, cst[h - 1])
            p.dma("sync", upf[:].rearrange("p (a m) -> p a m", m=1024),
                  fup[h].rearrange("(a p) m -> p a m", p=128), dWF)
            wld[h] = p.dma("sync", dnf[:].rearrange("p (a m) -> p a m", m=256),
                           fdn[h].rearrange("(a p) m -> p a m", p=128), dWF)
            p.wait("vector", dWF, wld[h])
            if h >= 2:
                p.wait("vector", tA, gcons[h - 2])
            p.op("vector", lambda e, q=r2: e.tensor_copy(upb[:, q * 2048:(q + 1) * 2048], upf[:]))
            cst[h] = p.op("vector", lambda e, q=r2: e.tensor_copy(dnb[:, q * 2048:(q + 1) * 2048], dnf[:]), vA)

        ffn_wload(0)
        for h in range(16):
            r = h % 2
            if h + 1 < 16:
                ffn_wload(h + 1)
            if h >= 2:
                p.wait("gpsimd", tA, gcons[h - 2])
            for k2 in range(2):
                gat[h] = p.gather("gpsimd", ypg[:, (r * 2 + k2) * 1024:(r * 2 + k2 + 1) * 1024],
                                  yscr.ap(), permb[:, 2 * h + k2:2 * h + k2 + 1], dGA2[r])
            glv = {}
            for mt in range(8):
                pp = mt % 2
                if mt == 0:
                    p.wait("tensor", dGA2[h % 2], gat[h])
                    p.wait("tensor", vA, cst.get(h + 1, cst[h]))
                if mt >= 2:
                    p.wait("tensor", aA, glv[mt - 2])
                for k2 in range(2):
                    for half in range(2):
                        p.op("tensor", lambda e, q=r, m=mt, k=k2, hf=half, pz=pp: e.matmul(
                            pup[:, pz * 1024 + hf * 512: pz * 1024 + (hf + 1) * 512],
                            upb[:, q * 2048 + 1024 * k + 128 * m: q * 2048 + 1024 * k + 128 * (m + 1)],
                            ypg[:, (q * 2 + k) * 1024 + hf * 512: (q * 2 + k) * 1024 + (hf + 1) * 512],
                            start=(k == 0), stop=(k == 1), skip_group_check=True),
                            tA if (k2 == 1 and half == 1) else None)
                mmv = pe_fence(pup[:, pp * 1024 + 512: pp * 1024 + 768], tb[0:1, 0:256], tA)
                p.wait("scalar", tA, mmv)
                if mt == 0 and h >= 1:
                    p.wait("scalar", tA, down_done[h - 1])
                glv[mt] = p.op("scalar", lambda e, m=mt, pz=pp: e.activation(
                    tb[:, 1024 * m:1024 * (m + 1)], pup[:, pz * 1024:(pz + 1) * 1024], AF.Gelu), aA)
            for dt in range(2):
                p.wait("tensor", aA, glv[7] if dt == 0 else glv[7])
                prev_dn = dcons.get((h, dt - 1), dcons.get((h - 1, 1)))
                p.wait("tensor", vA, prev_dn)
                for k8 in range(8):
                    for half in range(2):
                        p.op("tensor", lambda e, q=r, d=dt, k=k8, hf=half: e.matmul(
                            pdn[:, hf * 512:(hf + 1) * 512],
                            dnb[:, q * 2048 + 256 * k + 128 * d: q * 2048 + 256 * k + 128 * (d + 1)],
                            tb[:, 1024 * k + hf * 512: 1024 * k + (hf + 1) * 512],
                            start=(k == 0), stop=(k == 7), skip_group_check=True),
                            tA if (k8 == 7 and half == 1) else None)
                dnv = pe_fence(pdn[:, 512:768], tb[0:1, 0:256], tA)
                p.wait("vector", tA, dnv)
                p.wait("vector", dGS2[h % 2], dGS2[h % 2].n)
                dcons[(h, dt)] = p.op("vector", lambda e, q=r, d=dt: e.tensor_copy(
                    obf[:, (q * 2 + d) * 1024:(q * 2 + d + 1) * 1024], pdn[:]), vA)
                p.wait("gpsimd", vA, dcons[(h, dt)])
                scat[(h, dt)] = p.scatter("gpsimd", fscr.ap(), permb[:, 2 * h + dt:2 * h + dt + 1],
                                          obf[:, (r * 2 + dt) * 1024:(r * 2 + dt + 1) * 1024], dGS2[h % 2])
            down_done[h] = tA.n
            gcons[h] = tA.n
        all_scat = (dGS2[0].n, dGS2[1].n)

        z_v = {}
        p.wait("sync", dGS2[0], all_scat[0])
        p.wait("sync", dGS2[1], all_scat[1])
        for mt in range(KT):
            r2 = mt % 2
            if mt >= 2:
                p.wait("sync", vA, z_v[mt - 2])
            fv = p.dma("sync", ypg[:, r2 * 1024:(r2 + 1) * 1024], fscr.ap()[128 * mt:128 * (mt + 1), :], dF2[r2])
            p.wait("vector", dF2[r2], fv)
            z_v[mt] = p.op("vector", lambda e, m=mt, q=r2: e.tensor_tensor(
                xT[:, 1024 * m:1024 * (m + 1)], QA[:, 1024 * m:1024 * (m + 1)],
                ypg[:, q * 1024:(q + 1) * 1024], op=ALU.add), vA)
        z_all = z_v[KT - 1]

    _cut(7)

    phase_sync()
    with nc.sbuf_tensor("zt", [128, 2 * DM], BF16) as zt, \
         nc.sbuf_tensor("zq", [128, DM], BF16) as zq, \
         nc.sbuf_tensor("outf", [128, DM], F32) as outf, \
         nc.sbuf_tensor("l2s", [128, 8], F32) as l2s, \
         nc.psum_tensor("ptz", [128, 2048], BF16) as ptz:

        p.wait("tensor", vA, z_all)
        out_dma = {}
        apv = {}
        zq_red = {}
        ev_t = {}
        out_v = 0
        for tt in range(8):
            if tt >= KT7:
                p.muted = True
            r = tt % 2
            if tt >= 2:
                p.wait("tensor", vA, apv[tt - 2])
            for ft in range(KT):
                gidx = tt * KT + ft
                ps = gidx % 2
                if gidx >= 2:
                    p.wait("tensor", vA, ev_t[gidx - 2])
                trv = p.op("tensor", lambda e, f=ft, t=tt, s=ps: e.transpose(
                    ptz[:, 1024 * s:1024 * s + 128], xT[:, 1024 * f + 128 * t: 1024 * f + 128 * (t + 1)], identb[:]), tA)
                p.wait("vector", tA, trv)
                ev_t[gidx] = p.op("vector", lambda e, f=ft, s=ps, q=r: e.tensor_copy(
                    zt[:, q * DM + 128 * f: q * DM + 128 * (f + 1)], ptz[:, 1024 * s:1024 * s + 128]), vA)
            if tt == KT7 - 1 and KP7 < 1:
                p.muted = True
            zrow = zt[:, r * DM:(r + 1) * DM]
            zr0 = p.op("vector", lambda e, z=zrow: e.tensor_reduce(l2s[:, 0:1], z, axis=mybir.AxisListType.X, op=ALU.add), vA)
            p.wait("scalar", vA, ev_t[tt * KT + KT - 1])
            if tt >= 1:
                p.wait("scalar", vA, zq_red[tt - 1])
            sqz = p.op("scalar", lambda e, z=zrow: e.activation(zq[:], z, AF.Square), aA)
            p.wait("vector", aA, sqz)
            p.wait("vector", vA, zr0)
            zq_red[tt] = p.op("vector", lambda e: e.tensor_reduce(l2s[:, 1:2], zq[:], axis=mybir.AxisListType.X, op=ALU.add), vA)
            chain2 = [
                lambda e: e.tensor_scalar(l2s[:, 2:3], l2s[:, 0:1], 1.0 / DM, None, op0=ALU.mult),
                lambda e: e.tensor_scalar(l2s[:, 3:4], l2s[:, 1:2], 1.0 / DM, None, op0=ALU.mult),
                lambda e: e.tensor_tensor(l2s[:, 4:5], l2s[:, 2:3], l2s[:, 2:3], op=ALU.mult),
                lambda e: e.tensor_tensor(l2s[:, 4:5], l2s[:, 3:4], l2s[:, 4:5], op=ALU.subtract),
                lambda e: e.tensor_scalar(l2s[:, 4:5], l2s[:, 4:5], EPS, None, op0=ALU.add),
            ]
            svv = zq_red[tt]
            p.wait("vector", vA, svv)
            for fn in chain2:
                svv = p.op("vector", fn, vA)
                p.wait("vector", vA, svv)
            p.wait("scalar", vA, svv)
            sq2 = p.op("scalar", lambda e: e.activation(l2s[:, 5:6], l2s[:, 4:5], AF.Sqrt), aA)
            p.wait("vector", aA, sq2)
            rc2 = p.op("vector", lambda e: e.reciprocal(l2s[:, 5:6], l2s[:, 5:6]), vA)
            p.wait("vector", vA, rc2)
            nmu = p.op("vector", lambda e: e.tensor_scalar(l2s[:, 6:7], l2s[:, 2:3], -1.0, None, op0=ALU.mult), vA)
            p.wait("vector", vA, nmu)
            if tt == KT7 - 1 and KP7 < 2:
                p.muted = True
            p.wait("vector", dOUT, out_dma.get(tt - 1, 0))
            apv[tt] = p.op("vector", lambda e, z=zrow: e.tensor_scalar(
                outf[:], z, l2s[:, 6:7], l2s[:, 5:6], op0=ALU.add, op1=ALU.mult), vA)
            if tt == KT7 - 1 and KP7 < 3:
                p.muted = True
            p.wait("scalar", vA, apv[tt])
            out_dma[tt] = p.dma("scalar", out_d[128 * tt:128 * (tt + 1), :], outf[:], dOUT)
            out_v = out_dma[tt]

        p.wait("sync", dOUT, out_v)

    def play(eng_name):
        def f(e):
            for fn in p.s[eng_name]:
                fn(e)
        return f

    with nc.Block() as blk:
        blk.sync(play("sync"))
        blk.tensor(play("tensor"))
        blk.vector(play("vector"))
        blk.scalar(play("scalar"))
        blk.gpsimd(play("gpsimd"))

    es.close()
    return nc


_CACHE = {}


def _host_consts(core):
    ch = chunks_of(core)
    pos = np.concatenate([np.arange(256 * q, 256 * q + 256) for q in ch]).astype(np.float64)
    inv = 1.0 / (10000.0 ** (np.arange(64, dtype=np.float64) / 64.0))
    ang = pos[None, :] * inv[:, None]
    cost = np.concatenate([np.cos(ang), np.cos(ang)], axis=0).astype(np.float32)
    sint = np.concatenate([-np.sin(ang), np.sin(ang)], axis=0).astype(np.float32)
    bias = np.zeros((128, 128), np.float32)
    for i in range(4):
        Q = ch[i]
        for s in range(1, 32):
            if s - 1 >= Q:
                bias[:, 32 * i + s] = NEG
    return cost, sint, bias


def _shared_consts():
    trim = np.zeros((128, 2048), np.float32)
    pp = np.arange(128)
    qq = np.arange(256)
    for kt in range(2):
        m = (128 * kt + pp[:, None]) > qq[None, :]
        for h in range(4):
            trim[:, 1024 * kt + 256 * h:1024 * kt + 256 * (h + 1)] = np.where(m, NEG, 0.0).astype(np.float32)
    iden = np.eye(128, dtype=np.float32).astype(BF)
    return trim.astype(BF), iden


def kernel(L0_attn_x, L0_attn_q_codes, L0_attn_k_codes, L0_attn_v_codes, L0_attn_o_codes,
           L0_attn_q_scales, L0_attn_k_scales, L0_attn_v_scales, L0_attn_o_scales,
           L0_ffn_up, L0_ffn_down, L0_ffn_perm):
    x = np.asarray(L0_attn_x, dtype=np.float32).reshape(SEQ, DM)
    trim, iden = _shared_consts()
    perm = np.asarray(L0_ffn_perm, dtype=np.int32)
    permb = np.ascontiguousarray(perm.reshape(32, 128).T)

    if "nc" not in _CACHE:
        _CACHE["nc"] = build_program()
    nc = _CACHE["nc"]

    common = {
        "qcod": np.ascontiguousarray(np.asarray(L0_attn_q_codes, dtype=np.int32)),
        "kcod": np.ascontiguousarray(np.asarray(L0_attn_k_codes, dtype=np.int32)),
        "vcod": np.ascontiguousarray(np.asarray(L0_attn_v_codes, dtype=np.int32)),
        "ocod": np.ascontiguousarray(np.asarray(L0_attn_o_codes, dtype=np.int32)),
        "qscl": np.ascontiguousarray(np.asarray(L0_attn_q_scales, dtype=np.float32)),
        "kscl": np.ascontiguousarray(np.asarray(L0_attn_k_scales, dtype=np.float32)),
        "vscl": np.ascontiguousarray(np.asarray(L0_attn_v_scales, dtype=np.float32)),
        "oscl": np.ascontiguousarray(np.asarray(L0_attn_o_scales, dtype=np.float32)),
        "fup": np.ascontiguousarray(np.asarray(L0_ffn_up, dtype=np.float32)),
        "fdn": np.ascontiguousarray(np.asarray(L0_ffn_down, dtype=np.float32)),
        "permi": permb,
        "trim": trim,
        "iden": iden,
    }
    in_maps = []
    for c in range(NCORE):
        ch = chunks_of(c)
        xl = np.ascontiguousarray(np.concatenate([x[256 * q:256 * (q + 1)] for q in ch], axis=0))
        cost, sint, bias = _host_consts(c)
        m = dict(common)
        m["x_in"] = xl
        m["cost"] = cost
        m["sint"] = sint
        m["biasb"] = bias
        in_maps.append(m)

    res = run_bass_kernel_spmd(nc, in_maps, core_ids=list(range(NCORE)))

    out = np.empty((SEQ, DM), np.float32)
    for c in range(NCORE):
        r = res.results[c]["out"]
        ch = chunks_of(c)
        for i in range(4):
            out[256 * ch[i]:256 * (ch[i] + 1)] = r[256 * i:256 * (i + 1)]
    if DEBUG_TAPS:
        _CACHE["taps"] = [dict(res.results[c]) for c in range(NCORE)]
    return out.reshape(1, SEQ, DM)
